# version 10
# baseline (speedup 1.0000x reference)
import numpy as np
import concourse.bass as bass
import concourse.mybir as mybir
from concourse.bass_utils import run_bass_kernel_spmd
from contextlib import ExitStack

F32 = mybir.dt.float32
BF16 = mybir.dt.bfloat16
AF = mybir.ActivationFunctionType
ALU = mybir.AluOpType
AX = mybir.AxisListType

S = 2048
D = 2048
NH = 16
NT = 16
NC = 16
NBK = 4
EPS = 1e-5
ATT_SCALE = float(1.0 / np.sqrt(np.float32(128.0)))
INV_D = float(1.0 / 2048.0)

ENG = ("sync", "tensor", "vector", "scalar", "gpsimd")


class Sem:
    def __init__(self, stack, nc, name):
        self.h = stack.enter_context(nc.semaphore(name))
        self.n = 0


class EL:
    def __init__(self, block):
        self.block = block
        self.q = {e: [] for e in ENG}

    def wait(self, eng, sem, val=None):
        v = sem.n if val is None else val
        self.q[eng].append(lambda E, s=sem.h, v=v: E.wait_ge(s, v))

    def op(self, eng, fn, incs=()):
        frozen = tuple((s.h, k) for s, k in incs)
        for s, k in incs:
            s.n += k

        def run(E, fn=fn, frozen=frozen):
            ins = fn(E)
            for h, k in frozen:
                ins.then_inc(h, k)
        self.q[eng].append(run)

    def flush(self):
        for e in ENG:
            lst = self.q[e]
            if lst:
                getattr(self.block, e)(lambda E, lst=lst: [f(E) for f in lst])
        self.q = {e: [] for e in ENG}


class TPT:
    def __init__(self):
        self.i = 0
        self.ev_at = []


class Prog:
    def __init__(self, name):
        self.nc = bass.Bass("TRN2", target_bir_lowering=False, debug=False)
        self.stack = ExitStack()
        self.name = name
        self.sems = {}
        self.nobar = set()

    def sem(self, name):
        if name not in self.sems:
            self.sems[name] = Sem(self.stack, self.nc, name)
        return self.sems[name]

    def sb(self, name, shape, dt=F32):
        return self.stack.enter_context(self.nc.sbuf_tensor(name, shape, dt))

    def ps(self, name, shape, dt=F32):
        return self.stack.enter_context(self.nc.psum_tensor(name, shape, dt))

    def begin(self):
        self.block = self.stack.enter_context(self.nc.Block())
        self.el = EL(self.block)
        self.tpt = TPT()
        return self.block

    def finish(self):
        self.stack.close()
        return self.nc


def transpose_group(P, n, src_ap, dst_ap, eye_b, ps_tp, pre_t=(), pre_v=()):
    el, tpt = P.el, P.tpt
    tp, ev, axe = P.sem("tp"), P.sem("ev"), P.sem("axe")
    esem = {0: ev, 1: axe}
    base = {0: ev.n, 1: axe.n}
    base_idx = tpt.i
    cnt = {0: 0, 1: 0}
    for j in range(n):
        k = base_idx + j
        cnt[k % 2] += 1
        tpt.ev_at.append((esem[k % 2], base[k % 2] + cnt[k % 2]))
    for sem, val in pre_t:
        el.wait("tensor", sem, val)
    for j in range(n):
        k = base_idx + j
        if k >= 2:
            g = tpt.ev_at[k - 2]
            el.wait("tensor", g[0], g[1])
        slot = k % 2
        el.op("tensor",
              lambda E, s=slot, j=j: E.transpose(ps_tp[s][:, 0:128], src_ap(j), eye_b[:]),
              incs=[(tp, 1)])
    tp_after_first = tp.n - n + 1
    first = {0: True, 1: True}
    for j in range(n):
        k = base_idx + j
        eng = "vector" if k % 2 == 0 else "scalar"
        if first[k % 2]:
            for sem, val in pre_v:
                el.wait(eng, sem, val)
            first[k % 2] = False
        el.wait(eng, tp, tp_after_first + j)
        el.op(eng,
              lambda E, s=(k % 2), j=j: (E.tensor_copy(dst_ap(j), ps_tp[s][:, 0:128])
                                          if s == 0 else E.copy(dst_ap(j), ps_tp[s][:, 0:128])),
              incs=[(esem[k % 2], 1)])
    tpt.i += n
    return [(ev, ev.n), (axe, axe.n)]


def emit_layernorm(P, r_ap, trash_ap, xo_ap, sm, ev_r):
    el = P.el
    vx, ax, ev = P.sem("vx"), P.sem("ax"), P.sem("ev")
    el.wait("vector", ev, ev_r)
    el.op("vector", lambda E: E.reduce_sum(sm["ssum"], r_ap, axis=AX.X), incs=[(vx, 1)])
    vx_sum = vx.n
    el.wait("scalar", vx, vx_sum)
    el.op("scalar", lambda E: E.activation(trash_ap, r_ap, AF.Square, accum_out=sm["ssq"]),
          incs=[(ax, 1)])
    ax_sq = ax.n
    el.wait("vector", vx, vx_sum)
    el.op("vector", lambda E: E.tensor_scalar(sm["nm"], sm["ssum"], -INV_D, None, ALU.mult),
          incs=[(vx, 1)])
    el.wait("vector", ax, ax_sq)
    el.op("vector", lambda E: E.tensor_scalar(sm["vpe0"], sm["ssq"], INV_D, EPS, ALU.mult, ALU.add),
          incs=[(vx, 1)])
    el.wait("vector", vx)
    el.op("vector", lambda E: E.tensor_tensor(sm["mq"], sm["nm"], sm["nm"], ALU.mult),
          incs=[(vx, 1)])
    el.wait("vector", vx)
    el.op("vector", lambda E: E.tensor_tensor(sm["vpe"], sm["vpe0"], sm["mq"], ALU.subtract),
          incs=[(vx, 1)])
    el.wait("scalar", vx)
    el.op("scalar", lambda E: E.activation(sm["sqv"], sm["vpe"], AF.Sqrt), incs=[(ax, 1)])
    el.wait("vector", ax)
    el.op("vector", lambda E: E.reciprocal(sm["rstd"], sm["sqv"]), incs=[(vx, 1)])
    el.wait("vector", vx)
    el.op("vector", lambda E: E.tensor_scalar(xo_ap, r_ap, sm["nm"], sm["rstd"],
                                              ALU.add, ALU.mult), incs=[(vx, 1)])
    return vx.n, ax.n


def phase_barrier(P):
    el = P.el
    for eng in ENG:
        for name, sem in P.sems.items():
            if sem.n > 0 and name not in P.nobar:
                el.wait(eng, sem)


def smalls_of(t):
    names = ["ssum", "ssq", "nm", "vpe0", "mq", "vpe", "sqv", "rstd"]
    return {k: t[:, j:j + 1] for j, k in enumerate(names)}


def emit_build_xt(P, x_rows, bigA, xb, eye_b, ps_tp, mm_gate, consts_cld, gates=()):
    el = P.el
    cld, tp, mm, ev = P.sem("cld"), P.sem("tp"), P.sem("mm"), P.sem("ev")
    ld0 = cld.n
    tp0 = tp.n
    for t in range(NT):
        if t == 0:
            el.wait("gpsimd", mm, mm_gate)
            for s, v in gates:
                el.wait("gpsimd", s, v)
        if t >= 2:
            el.wait("gpsimd", tp, tp0 + 16 * (t - 1))
        el.op("gpsimd", lambda E, t=t: E.dma_start(
            out=xb[t % 2][:], in_=x_rows[t * 128:(t + 1) * 128, :]), incs=[(cld, 16)])
    done = None
    for t in range(NT):
        pre_t = [(cld, ld0 + 16 * (t + 1))]
        if t == 0:
            pre_t.append((cld, consts_cld))
        done = transpose_group(
            P, NC,
            src_ap=lambda c, t=t: xb[t % 2][:, c * 128:(c + 1) * 128],
            dst_ap=lambda c, t=t: bigA[:, c * S + t * 128: c * S + (t + 1) * 128],
            eye_b=eye_b, ps_tp=ps_tp,
            pre_t=pre_t,
            pre_v=([(mm, mm_gate)] if t == 0 else ()))
    return done


def build_attn_program(packed, debug=False):
    P = Prog("attn")
    nc = P.nc
    xin = nc.dram_tensor("xin", [2, S, D], F32, kind="ExternalInput").ap()
    if packed:
        wqkv = nc.dram_tensor("wqkv", [D, 3 * D], F32, kind="ExternalInput").ap()
        w_aps = [wqkv[:, 0:D], wqkv[:, D:2 * D], wqkv[:, 2 * D:3 * D]]
    else:
        w_aps = [nc.dram_tensor(n, [D, D], F32, kind="ExternalInput").ap()
                 for n in ("wq", "wk", "wv")]
    wo_ap = nc.dram_tensor("wo", [D, D], F32, kind="ExternalInput").ap()
    eye_in = nc.dram_tensor("eye", [128, 128], F32, kind="ExternalInput").ap()
    mask_in = nc.dram_tensor("mask", [128, 128], F32, kind="ExternalInput").ap()
    mask01_in = nc.dram_tensor("mask01", [128, 128], F32, kind="ExternalInput").ap()
    xout = nc.dram_tensor("xout", [2, S, D], F32, kind="ExternalOutput").ap()
    qt_hbm = nc.dram_tensor("qt_i", [D, S], BF16,
                            kind="ExternalOutput" if debug else "Internal").ap()
    if debug:
        dbg_xt = nc.dram_tensor("xt_dbg", [128, NC * S], BF16, kind="ExternalOutput").ap()
        dbg_ctx = nc.dram_tensor("ctx_dbg", [128, NT * D], BF16, kind="ExternalOutput").ap()
    else:
        dbg_xt = dbg_ctx = None
    kt_hbm = nc.dram_tensor("kt_i", [D, S], BF16,
                            kind="ExternalOutput" if debug else "Internal").ap()
    v_hbm = nc.dram_tensor("v_i", [S, D], BF16,
                           kind="ExternalOutput" if debug else "Internal").ap()

    bigA = P.sb("bigA", [128, NC * S], BF16)
    bigB = P.sb("bigB", [128, NT * D], BF16)
    eye_b = P.sb("eye_b", [128, 128], BF16)
    mask_f = P.sb("mask_f", [128, 128], F32)
    mask01_b = P.sb("mask01_b", [128, 128], BF16)
    ps_blk = P.ps("ps_blk", [128, 2048], F32)
    ps_tp = [P.ps("ps_tp0", [128, 512], BF16), P.ps("ps_tp1", [128, 512], BF16)]
    ps_ctx = [P.ps("ps_ctx0", [128, 512], F32), P.ps("ps_ctx1", [128, 512], F32)]

    block = P.begin()
    el = P.el
    cld, sld, sst = P.sem("cld"), P.sem("sld"), P.sem("sst")
    mm, tp, ev, vx, ax = P.sem("mm"), P.sem("tp"), P.sem("ev"), P.sem("vx"), P.sem("ax")

    el.op("sync", lambda E: E.dma_start(out=mask_f[:], in_=mask_in[:, :]), incs=[(sld, 16)])
    el.op("gpsimd", lambda E: E.dma_start(out=eye_b[:], in_=eye_in[:, :]), incs=[(cld, 16)])
    el.op("gpsimd", lambda E: E.dma_start(out=mask01_b[:], in_=mask01_in[:, :]), incs=[(cld, 16)])
    consts_sld, consts_cld = sld.n, cld.n
    el.flush()

    state = {"mm_gate_bigA": 0, "ctx_gate_tp": 0, "dbg_xt": dbg_xt, "dbg_ctx": dbg_ctx,
             "psgate": {0: None, 1: None, 2: None, 3: None}, "mask01_b": mask01_b}
    for e in range(2):
        _emit_attn_elem(P, e, xin[e], w_aps, wo_ap, xout[e], qt_hbm, kt_hbm, v_hbm,
                        bigA, bigB, eye_b, mask_f, ps_blk, ps_tp, ps_ctx,
                        consts_sld, consts_cld, state)

    el.wait("sync", sst)
    el.flush()
    return P


def _emit_attn_elem(P, e, x_e, w_aps, wo_ap, xout_e, qt_hbm, kt_hbm, v_hbm,
                    bigA, bigB, eye_b, mask_f, ps_blk, ps_tp, ps_ctx,
                    consts_sld, consts_cld, state, tagp=""):
    nc, el = P.nc, P.el
    cld, sld, sst = P.sem("cld"), P.sem("sld"), P.sem("sst")
    mm, tp, ev, vx, ax = P.sem("mm"), P.sem("tp"), P.sem("ev"), P.sem("vx"), P.sem("ax")
    tag = f"{tagp}e{e}"

    phase_barrier(P)
    with nc.sbuf_tensor(tag + "_xb0", [128, D], BF16) as xb0, \
         nc.sbuf_tensor(tag + "_xb1", [128, D], BF16) as xb1:
        xt_done = emit_build_xt(P, x_e, bigA, [xb0, xb1], eye_b, ps_tp,
                                 state["mm_gate_bigA"], consts_cld,
                                 gates=state.get("elem_gates", ()))
        if e == 0 and state.get("dbg_xt") is not None:
            for sm_, val_ in xt_done:
                el.wait("sync", sm_, val_)
            el.op("sync", lambda E: E.dma_start(out=state["dbg_xt"][:, :], in_=bigA[:]),
                  incs=[(sst, 16)])
        el.flush()

        for (w_ap, out_hbm, nm) in [(w_aps[0], qt_hbm, "q"), (w_aps[1], kt_hbm, "k")]:
            phase_barrier(P)
            with nc.sbuf_tensor(f"{tag}_wc{nm}0", [128, NC * 128], BF16) as wc0, \
                 nc.sbuf_tensor(f"{tag}_wc{nm}1", [128, NC * 128], BF16) as wc1, \
                 nc.sbuf_tensor(f"{tag}_ln{nm}0", [128, S], BF16) as ln0, \
                 nc.sbuf_tensor(f"{tag}_ln{nm}1", [128, S], BF16) as ln1:
                wc, lnb = [wc0, wc1], [ln0, ln1]
                w_r = w_ap.rearrange("(c p) o -> p c o", p=128)
                wld0, mm0, ev0, st0 = cld.n, mm.n, ev.n, sst.n
                wg8 = state.get("wload_gate")
                if wg8 is not None:
                    el.wait("gpsimd", wg8[0], wg8[1])
                for oc in range(NC):
                    if oc >= 2:
                        el.wait("gpsimd", mm, mm0 + (oc - 1) * NBK)
                    el.op("gpsimd", lambda E, oc=oc: E.dma_start(
                        out=wc[oc % 2][:], in_=w_r[:, :, oc * 128:(oc + 1) * 128]),
                        incs=[(cld, 16)])
                for sm_, val_ in xt_done:
                    el.wait("tensor", sm_, val_)
                psgate = state["psgate"]
                for oc in range(NC):
                    el.wait("tensor", cld, wld0 + 16 * (oc + 1))
                    for blk in range(NBK):
                        bidx = oc * NBK + blk
                        g = psgate[bidx % 4]
                        if g is not None:
                            el.wait("tensor", g[0], g[1])
                        psgate[bidx % 4] = (ev, ev0 + bidx + 1)
                        for c in range(NC):
                            el.op("tensor", lambda E, oc=oc, blk=blk, bidx=bidx, c=c: E.matmul(
                                ps_blk[:, (bidx % 4) * 512:(bidx % 4) * 512 + 512],
                                wc[oc % 2][:, c * 128:(c + 1) * 128],
                                bigA[:, c * S + blk * 512: c * S + blk * 512 + 512],
                                start=(c == 0), stop=(c == NC - 1)),
                                incs=([(mm, 1)] if c == NC - 1 else ()))
                for oc in range(NC):
                    for blk in range(NBK):
                        bidx = oc * NBK + blk
                        if blk == 0 and oc >= 2:
                            el.wait("vector", sst, st0 + 16 * (oc - 1))
                        el.wait("vector", mm, mm0 + bidx + 1)
                        el.op("vector", lambda E, oc=oc, blk=blk, bidx=bidx: E.tensor_copy(
                            lnb[oc % 2][:, blk * 512:(blk + 1) * 512],
                            ps_blk[:, (bidx % 4) * 512:(bidx % 4) * 512 + 512]),
                            incs=[(ev, 1)])
                for oc in range(NC):
                    el.wait("sync", ev, ev0 + (oc + 1) * NBK)
                    el.op("sync", lambda E, oc=oc: E.dma_start(
                        out=out_hbm[oc * 128:(oc + 1) * 128, :], in_=lnb[oc % 2][:]),
                        incs=[(sst, 16)])
                el.flush()

        phase_barrier(P)
        with nc.sbuf_tensor(tag + "_wv0", [128, NC * 512], BF16) as wv0, \
             nc.sbuf_tensor(tag + "_wv1", [128, NC * 512], BF16) as wv1, \
             nc.sbuf_tensor(tag + "_lv0", [128, 512], BF16) as lv0, \
             nc.sbuf_tensor(tag + "_lv1", [128, 512], BF16) as lv1:
            wv, lvb = [wv0, wv1], [lv0, lv1]
            w_r = w_aps[2].rearrange("(c p) o -> p c o", p=128)
            wld0, mm0, ev0, st0 = cld.n, mm.n, ev.n, sst.n
            wg8 = state.get("wload_gate")
            if wg8 is not None:
                el.wait("gpsimd", wg8[0], wg8[1])
            for ob in range(NBK):
                if ob >= 2:
                    el.wait("gpsimd", mm, mm0 + (ob - 1) * NT)
                el.op("gpsimd", lambda E, ob=ob: E.dma_start(
                    out=wv[ob % 2][:], in_=w_r[:, :, ob * 512:(ob + 1) * 512]),
                    incs=[(cld, 16)])
            psgate = state["psgate"]
            for ob in range(NBK):
                el.wait("tensor", cld, wld0 + 16 * (ob + 1))
                for t in range(NT):
                    bidx = ob * NT + t
                    g = psgate[bidx % 4]
                    if g is not None:
                        el.wait("tensor", g[0], g[1])
                    psgate[bidx % 4] = (ev, ev0 + bidx + 1)
                    for c in range(NC):
                        el.op("tensor", lambda E, ob=ob, t=t, bidx=bidx, c=c: E.matmul(
                            ps_blk[:, (bidx % 4) * 512:(bidx % 4) * 512 + 512],
                            bigA[:, c * S + t * 128: c * S + t * 128 + 128],
                            wv[ob % 2][:, c * 512:(c + 1) * 512],
                            start=(c == 0), stop=(c == NC - 1)),
                            incs=([(mm, 1)] if c == NC - 1 else ()))
            for ob in range(NBK):
                for t in range(NT):
                    bidx = ob * NT + t
                    if bidx >= 2:
                        el.wait("vector", sst, st0 + 16 * (bidx - 1))
                    el.wait("vector", mm, mm0 + bidx + 1)
                    el.op("vector", lambda E, bidx=bidx: E.tensor_copy(
                        lvb[bidx % 2][:], ps_blk[:, (bidx % 4) * 512:(bidx % 4) * 512 + 512]),
                        incs=[(ev, 1)])
            for ob in range(NBK):
                for t in range(NT):
                    bidx = ob * NT + t
                    el.wait("sync", ev, ev0 + bidx + 1)
                    el.op("sync", lambda E, ob=ob, t=t, bidx=bidx: E.dma_start(
                        out=v_hbm[t * 128:(t + 1) * 128, ob * 512:(ob + 1) * 512],
                        in_=lvb[bidx % 2][:]), incs=[(sst, 16)])
            el.flush()
    a2_mm, a2_sst, a2_ev, a2_ax = mm.n, sst.n, ev.n, ax.n

    phase_barrier(P)
    with nc.sbuf_tensor(tag + "_qh0", [128, S], BF16) as qh0, \
         nc.sbuf_tensor(tag + "_qh1", [128, S], BF16) as qh1, \
         nc.sbuf_tensor(tag + "_kh0", [128, S], BF16) as kh0, \
         nc.sbuf_tensor(tag + "_kh1", [128, S], BF16) as kh1, \
         nc.sbuf_tensor(tag + "_vh0", [128, S], BF16) as vh0, \
         nc.sbuf_tensor(tag + "_vh1", [128, S], BF16) as vh1, \
         nc.sbuf_tensor(tag + "_p0", [128, S], BF16) as p0, \
         nc.sbuf_tensor(tag + "_p1", [128, S], BF16) as p1, \
         nc.sbuf_tensor(tag + "_pt", [128, S], BF16) as pt_sb, \
         nc.sbuf_tensor(tag + "_sq", [128, S], BF16) as sqb, \
         nc.sbuf_tensor(tag + "_nsb", [1, 16], BF16) as nsb, \
         nc.sbuf_tensor(tag + "_on1", [128, 128], BF16) as on1, \
         nc.sbuf_tensor(tag + "_bs", [128, 24], F32) as bsm:
        qh, kh, vh, pb = [qh0, qh1], [kh0, kh1], [vh0, vh1], [p0, p1]
        v_r = v_hbm.rearrange("(c p) o -> p c o", p=128)
        axe = P.sem("axe")
        psgate = state["psgate"]
        mask01_b = state["mask01_b"]

        el.op("vector", lambda E: E.memset(on1[:], 1.0), incs=[(vx, 1)])
        on1_vx = vx.n

        pv_done = []
        pv_mms = []
        den_vxs = []
        exp_ax = []
        tpend = []
        ctx_evs = []
        sc_mms = {}
        mask_vxs = {}
        head_bias_vx = [None] * NH

        def emit_scores(h, i, head_ld, headbase_ax):
            gi = h * NT + i
            kmax = (i + 1) * 128
            nblk = (kmax + 511) // 512
            if i == 0:
                el.wait("tensor", sld, head_ld)
            for b in range(nblk):
                g = psgate[b]
                if g is not None:
                    el.wait("tensor", g[0], g[1])
                psgate[b] = (ax, headbase_ax + 3 + i + 1)
            for blk in range(nblk):
                w = min(512, kmax - blk * 512)
                el.op("tensor", lambda E, h=h, i=i, blk=blk, w=w: E.matmul(
                    ps_blk[:, blk * 512: blk * 512 + w],
                    qh[h % 2][:, i * 128:(i + 1) * 128],
                    kh[h % 2][:, blk * 512: blk * 512 + w],
                    start=True, stop=True),
                    incs=([(mm, 1)] if blk == nblk - 1 else ()))
            sc_mms[gi] = mm.n

        el.wait("sync", sst, a2_sst)
        head_lds = []
        norm_mms = [None] * NH
        for h in range(NH):
            if h >= 2:
                el.wait("sync", mm, pv_done[h - 2])
            el.op("sync", lambda E, h=h: E.dma_start(
                out=qh[h % 2][:], in_=qt_hbm[h * 128:(h + 1) * 128, :]), incs=[(sld, 16)])
            el.op("sync", lambda E, h=h: E.dma_start(
                out=kh[h % 2][:], in_=kt_hbm[h * 128:(h + 1) * 128, :]), incs=[(sld, 16)])
            el.op("sync", lambda E, h=h: E.dma_start(
                out=vh[h % 2][:], in_=v_r[:, :, h * 128:(h + 1) * 128]), incs=[(sld, 16)])
            head_lds.append(sld.n)
            headbase_ax = ax.n

            el.wait("scalar", sld, head_lds[h])
            if h >= 1 and norm_mms[h - 1] is not None:
                el.wait("scalar", mm, norm_mms[h - 1])
            el.op("scalar", lambda E, h=h: E.activation(sqb[:], qh[h % 2][:], AF.Square),
                  incs=[(ax, 1)])
            sqq_ax = ax.n
            el.wait("tensor", ax, sqq_ax)
            el.wait("tensor", vx, on1_vx)
            for blk in range(NBK):
                g = psgate[blk]
                if g is not None:
                    el.wait("tensor", g[0], g[1])
                el.op("tensor", lambda E, blk=blk: E.matmul(
                    ps_blk[0:1, blk * 512:(blk + 1) * 512], on1[:, 0:1],
                    sqb[:, blk * 512:(blk + 1) * 512], start=True, stop=True),
                    incs=([(mm, 1)] if blk == NBK - 1 else ()))
            mm_q = mm.n
            el.wait("vector", mm, mm_q)
            el.op("vector", lambda E: E.reduce_max(bsm[0:1, 0:1], ps_blk[0:1, 0:2048],
                                                   axis=AX.X), incs=[(vx, 1)])
            rq_vx = vx.n
            el.wait("scalar", mm, mm_q)
            el.op("scalar", lambda E, h=h: E.activation(sqb[:], kh[h % 2][:], AF.Square),
                  incs=[(ax, 1)])
            sqk_ax = ax.n
            el.wait("tensor", ax, sqk_ax)
            el.wait("tensor", vx, rq_vx)
            for blk in range(NBK):
                el.op("tensor", lambda E, blk=blk: E.matmul(
                    ps_blk[0:1, blk * 512:(blk + 1) * 512], on1[:, 0:1],
                    sqb[:, blk * 512:(blk + 1) * 512], start=True, stop=True),
                    incs=([(mm, 1)] if blk == NBK - 1 else ()))
            norm_mms[h] = mm.n
            el.wait("vector", mm, norm_mms[h])
            el.op("vector", lambda E: E.reduce_max(bsm[0:1, 1:2], ps_blk[0:1, 0:2048],
                                                   axis=AX.X), incs=[(vx, 1)])
            el.wait("vector", vx)
            el.op("vector", lambda E: E.tensor_tensor(bsm[0:1, 2:3], bsm[0:1, 0:1],
                                                      bsm[0:1, 1:2], ALU.mult), incs=[(vx, 1)])
            el.wait("scalar", vx)
            el.op("scalar", lambda E: E.activation(bsm[0:1, 3:4], bsm[0:1, 2:3], AF.Sqrt),
                  incs=[(ax, 1)])
            el.wait("vector", ax)
            el.op("vector", lambda E: E.tensor_scalar(nsb[0:1, 0:1], bsm[0:1, 3:4],
                                                      -ATT_SCALE, None, ALU.mult),
                  incs=[(vx, 1)])
            bias_src_vx = vx.n
            el.wait("tensor", vx, bias_src_vx)
            el.op("tensor", lambda E: E.matmul(ps_blk[:, 0:1], on1[0:1, 0:128],
                                               nsb[0:1, 0:1], start=True, stop=True),
                  incs=[(mm, 1)])
            bc_mm = mm.n
            el.wait("vector", mm, bc_mm)
            el.op("vector", lambda E: E.tensor_copy(bsm[:, 5:6], ps_blk[:, 0:1]),
                  incs=[(vx, 1)])
            head_bias_vx[h] = vx.n
            for b in range(NBK):
                psgate[b] = (vx, head_bias_vx[h])

            emit_scores(h, 0, head_lds[h], headbase_ax)

            for i in range(NT):
                kmax = (i + 1) * 128
                nblk = (kmax + 511) // 512
                gi = h * NT + i
                vw_last = min(512, kmax - (nblk - 1) * 512) - 128
                nvalid = (nblk - 1) + (1 if vw_last > 0 else 0)

                ab = 6 + 5 * (gi % 2)
                dsc, rdc = 16 + gi % 2, 18 + gi % 2
                el.wait("scalar", mm, sc_mms[gi])
                if gi >= 2:
                    el.wait("scalar", tp, tpend[gi - 2])
                    el.wait("scalar", vx, den_vxs[gi - 2])
                for blk in range(nblk - 1):
                    el.op("scalar", lambda E, gi=gi, blk=blk, ab=ab: E.activation(
                        pb[gi % 2][:, blk * 512: blk * 512 + 512],
                        ps_blk[:, blk * 512: blk * 512 + 512],
                        AF.Exp, bias=bsm[:, 5:6], scale=ATT_SCALE,
                        accum_out=bsm[:, ab + blk:ab + blk + 1]))
                if vw_last > 0:
                    b0 = (nblk - 1) * 512
                    el.op("scalar", lambda E, gi=gi, b0=b0, vw=vw_last, dc=ab + nblk - 1:
                          E.activation(pb[gi % 2][:, b0: b0 + vw], ps_blk[:, b0: b0 + vw],
                                       AF.Exp, bias=bsm[:, 5:6], scale=ATT_SCALE,
                                       accum_out=bsm[:, dc:dc + 1]))
                el.op("scalar", lambda E, gi=gi, kmax=kmax: E.activation(
                    pb[gi % 2][:, kmax - 128: kmax], ps_blk[:, kmax - 128: kmax],
                    AF.Exp, bias=bsm[:, 5:6], scale=ATT_SCALE),
                    incs=[(ax, 1)])
                exp_ax.append(ax.n)
                assert ax.n == headbase_ax + 3 + i + 1

                if i + 1 < NT:
                    emit_scores(h, i + 1, head_lds[h], headbase_ax)

                el.wait("vector", ax, exp_ax[gi])
                el.op("vector", lambda E, gi=gi, kmax=kmax: E.tensor_tensor(
                    pb[gi % 2][:, kmax - 128:kmax], pb[gi % 2][:, kmax - 128:kmax],
                    mask01_b[:], ALU.mult), incs=[(vx, 1)])
                mask_vx = vx.n
                el.wait("vector", vx)
                el.op("vector", lambda E, gi=gi, kmax=kmax, dc=ab + nvalid: E.reduce_sum(
                    bsm[:, dc:dc + 1], pb[gi % 2][:, kmax - 128:kmax], axis=AX.X),
                    incs=[(vx, 1)])
                el.wait("vector", vx)
                if nvalid + 1 > 1:
                    el.op("vector", lambda E, np_=nvalid + 1, ab=ab, dsc=dsc: E.tensor_reduce(
                        bsm[:, dsc:dsc + 1], bsm[:, ab:ab + np_], axis=AX.X, op=ALU.add),
                        incs=[(vx, 1)])
                    el.wait("vector", vx)
                    el.op("vector", lambda E, dsc=dsc, rdc=rdc: E.reciprocal(
                        bsm[:, rdc:rdc + 1], bsm[:, dsc:dsc + 1]), incs=[(vx, 1)])
                else:
                    el.op("vector", lambda E, dc=ab + nvalid, rdc=rdc: E.reciprocal(
                        bsm[:, rdc:rdc + 1], bsm[:, dc:dc + 1]), incs=[(vx, 1)])
                den_vxs.append(vx.n)
                rden_vx = vx.n

                nch = i + 1
                pre_v = [(mm, pv_mms[gi - 1])] if gi >= 1 else []
                if nch > 1:
                    transpose_group(
                        P, nch - 1,
                        src_ap=lambda c, gi=gi: pb[gi % 2][:, c * 128:(c + 1) * 128],
                        dst_ap=lambda c: pt_sb[:, c * 128:(c + 1) * 128],
                        eye_b=eye_b, ps_tp=ps_tp, pre_t=[(ax, exp_ax[gi])], pre_v=pre_v)
                    pre_v = []
                done = transpose_group(
                    P, 1,
                    src_ap=lambda c, gi=gi, i=i: pb[gi % 2][:, i * 128:(i + 1) * 128],
                    dst_ap=lambda c, i=i: pt_sb[:, i * 128:(i + 1) * 128],
                    eye_b=eye_b, ps_tp=ps_tp, pre_t=[(vx, mask_vx)], pre_v=pre_v)
                tpend.append(tp.n)

                for sm_, val_ in done:
                    el.wait("tensor", sm_, val_)
                if gi >= 2:
                    el.wait("tensor", ev, ctx_evs[gi - 2])
                for c in range(nch):
                    el.op("tensor", lambda E, h=h, gi=gi, c=c, nch=nch: E.matmul(
                        ps_ctx[gi % 2][:, 0:128],
                        pt_sb[:, c * 128:(c + 1) * 128],
                        vh[h % 2][:, c * 128:(c + 1) * 128],
                        start=(c == 0), stop=(c == nch - 1)),
                        incs=([(mm, 1)] if c == nch - 1 else ()))
                pv_mm = mm.n
                pv_mms.append(pv_mm)
                if i == NT - 1:
                    pv_done.append(mm.n)

                el.wait("vector", mm, pv_mm)
                el.wait("vector", vx, rden_vx)
                if gi == 0:
                    el.wait("vector", tp, state["ctx_gate_tp"])
                el.op("vector", lambda E, h=h, i=i, gi=gi, rdc=18 + gi % 2: E.tensor_scalar(
                    bigB[:, i * D + h * 128: i * D + (h + 1) * 128],
                    ps_ctx[gi % 2][:, 0:128], bsm[:, rdc:rdc + 1], None, ALU.mult),
                    incs=[(ev, 1)])
                ctx_evs.append(ev.n)
        el.flush()
    b_ev, b_ax = ev.n, ax.n
    if False and e == 0 and state.get("dbg_ctx") is not None:
        el.wait("sync", ev, b_ev)
        el.op("sync", lambda E: E.dma_start(out=state["dbg_ctx"][:, :], in_=bigB[:]),
              incs=[(sst, 16)])

    phase_barrier(P)
    with nc.sbuf_tensor(tag + "_ct0", [128, D], BF16) as ct0, \
         nc.sbuf_tensor(tag + "_ct1", [128, D], BF16) as ct1, \
         nc.sbuf_tensor(tag + "_xr0", [128, D], F32) as xr0, \
         nc.sbuf_tensor(tag + "_xr1", [128, D], F32) as xr1, \
         nc.sbuf_tensor(tag + "_r0", [128, D], F32) as r0, \
         nc.sbuf_tensor(tag + "_r1", [128, D], F32) as r1, \
         nc.sbuf_tensor(tag + "_xo0", [128, D], F32) as xo0, \
         nc.sbuf_tensor(tag + "_xo1", [128, D], F32) as xo1, \
         nc.sbuf_tensor(tag + "_smt", [128, 8], F32) as smt:
        ct, xr, rr, xo = [ct0, ct1], [xr0, xr1], [r0, r1], [xo0, xo1]
        sm = smalls_of(smt)

        el.wait("gpsimd", mm, a2_mm)
        wg8 = state.get("wload_gate")
        if wg8 is not None:
            el.wait("gpsimd", wg8[0], wg8[1])
        for c in range(NC):
            el.op("gpsimd", lambda E, c=c: E.dma_start(
                out=bigA[:, c * S:(c + 1) * S], in_=wo_ap[c * 128:(c + 1) * 128, :]),
                incs=[(cld, 16)])
        wo_ld = cld.n

        proj_mms = []
        resid_evs = {}
        sq_axs = []
        store_ssts = []
        el.wait("sync", mm)
        el.wait("sync", ev, b_ev)
        for t in range(NT):
            if t >= 2:
                el.wait("sync", ax, sq_axs[t - 2])
            el.op("sync", lambda E, t=t: E.dma_start(
                out=xr[t % 2][:], in_=x_e[t * 128:(t + 1) * 128, :]), incs=[(sld, 16)])
            xr_ld = sld.n

            pre_t = []
            if t == 0:
                pre_t = [(ev, b_ev)]
            pre_v = []
            if t >= 2:
                pre_v = [(mm, proj_mms[t - 2])]
            ct_done = transpose_group(
                P, NC,
                src_ap=lambda c, t=t: bigB[:, t * D + c * 128: t * D + (c + 1) * 128],
                dst_ap=lambda c, t=t: ct[t % 2][:, c * 128:(c + 1) * 128],
                eye_b=eye_b, ps_tp=ps_tp, pre_t=pre_t, pre_v=pre_v)
            ct_ev = ev.n

            if t == 0:
                el.wait("tensor", cld, wo_ld)
            for sm_, val_ in ct_done:
                el.wait("tensor", sm_, val_)
            psgate = state["psgate"]
            for ob in range(NBK):
                g = psgate[ob]
                if g is not None:
                    el.wait("tensor", g[0], g[1])
                psgate[ob] = (ev, ct_ev + ob + 1)
                for c in range(NC):
                    el.op("tensor", lambda E, t=t, ob=ob, c=c: E.matmul(
                        ps_blk[:, ob * 512: ob * 512 + 512],
                        ct[t % 2][:, c * 128:(c + 1) * 128],
                        bigA[:, c * S + ob * 512: c * S + ob * 512 + 512],
                        start=(c == 0), stop=(c == NC - 1)),
                        incs=([(mm, 1)] if c == NC - 1 else ()))
            proj_mms.append(mm.n)

            el.wait("vector", sld, xr_ld)
            for ob in range(NBK):
                el.wait("vector", mm, proj_mms[t] - NBK + ob + 1)
                el.op("vector", lambda E, t=t, ob=ob: E.tensor_tensor(
                    rr[t % 2][:, ob * 512:(ob + 1) * 512],
                    ps_blk[:, ob * 512: ob * 512 + 512],
                    xr[t % 2][:, ob * 512:(ob + 1) * 512], ALU.add),
                    incs=[(ev, 1)])
                resid_evs[(t, ob)] = ev.n

            if t >= 2:
                el.wait("vector", sst, store_ssts[t - 2])
            vx_apply, ax_sq = emit_layernorm(
                P, rr[t % 2][:], xr[t % 2][:], xo[t % 2][:], sm, resid_evs[(t, NBK - 1)])
            sq_axs.append(ax_sq)

            el.wait("sync", vx, vx_apply)
            el.op("sync", lambda E, t=t: E.dma_start(
                out=xout_e[t * 128:(t + 1) * 128, :], in_=xo[t % 2][:]), incs=[(sst, 16)])
            store_ssts.append(sst.n)
        el.flush()

    state["mm_gate_bigA"] = mm.n
    state["ctx_gate_tp"] = tp.n
    state["elem_gates"] = [(sst, sst.n), (vx, vx.n), (ax, ax.n)]


def build_ffn0_program():
    P = Prog("ffn0")
    nc = P.nc
    xin = nc.dram_tensor("xin", [2, S, D], F32, kind="ExternalInput").ap()
    w0t = nc.dram_tensor("w0t", [D, D], F32, kind="ExternalInput").ap()
    bias_in = nc.dram_tensor("bias_bc", [128, D], F32, kind="ExternalInput").ap()
    eye_in = nc.dram_tensor("eye", [128, 128], F32, kind="ExternalInput").ap()
    ones_in = nc.dram_tensor("ones", [128, 128], F32, kind="ExternalInput").ap()
    xout = nc.dram_tensor("xout", [2, S, D], F32, kind="ExternalOutput").ap()

    bigA = P.sb("bigA", [128, NC * S], BF16)
    bigB = P.sb("bigB", [128, NC * D], BF16)
    eye_b = P.sb("eye_b", [128, 128], BF16)
    ones_b = P.sb("ones_b", [128, 128], BF16)
    ones_f = P.sb("ones_f", [128, 128], F32)
    bias_f = P.sb("bias_f", [128, D], F32)
    alpha_bc = P.sb("alpha_bc", [128, D], F32)
    ps_blk = P.ps("ps_blk", [128, 2048], F32)
    ps_tp = [P.ps("ps_tp0", [128, 512], BF16), P.ps("ps_tp1", [128, 512], BF16)]

    block = P.begin()
    el = P.el
    cld, sld, sst = P.sem("cld"), P.sem("sld"), P.sem("sst")
    mm, tp, ev, vx, ax = P.sem("mm"), P.sem("tp"), P.sem("ev"), P.sem("vx"), P.sem("ax")

    el.op("sync", lambda E: E.dma_start(out=bias_f[:], in_=bias_in[:, :]), incs=[(sld, 16)])
    el.op("sync", lambda E: E.dma_start(out=ones_f[:], in_=ones_in[:, :]), incs=[(sld, 16)])
    el.op("gpsimd", lambda E: E.dma_start(out=eye_b[:], in_=eye_in[:, :]), incs=[(cld, 16)])
    el.op("gpsimd", lambda E: E.dma_start(out=ones_b[:], in_=ones_in[:, :]), incs=[(cld, 16)])
    consts_sld, consts_cld = sld.n, cld.n
    el.flush()

    state = {"mm_gate_bigA": 0, "psgate": {0: None, 1: None, 2: None, 3: None},
             "elem_gates": ()}
    _emit_ffn0_body(P, xin, xout, w0t, bigA, bigB, eye_b, ones_b, ones_f,
                    bias_f, alpha_bc, ps_blk, ps_tp, consts_sld, consts_cld, state)

    el.wait("sync", sst)
    el.flush()
    return P


def _emit_ffn0_body(P, xin, xout, w0t, bigA, bigB, eye_b, ones_b, ones_f,
                    bias_f, alpha_bc, ps_blk, ps_tp, consts_sld, consts_cld, state):
    nc, el = P.nc, P.el
    cld, sld, sst = P.sem("cld"), P.sem("sld"), P.sem("sst")
    mm, tp, ev, vx, ax = P.sem("mm"), P.sem("tp"), P.sem("ev"), P.sem("vx"), P.sem("ax")
    for e in range(2):
        tag = f"f{e}"
        phase_barrier(P)
        with nc.sbuf_tensor(tag + "_xb0", [128, D], BF16) as xb0, \
             nc.sbuf_tensor(tag + "_xb1", [128, D], BF16) as xb1:
            emit_build_xt(P, xin[e], bigA, [xb0, xb1], eye_b, ps_tp,
                          state["mm_gate_bigA"], consts_cld, gates=state["elem_gates"])
            el.flush()

            if e == 0:
                phase_barrier(P)
                with nc.sbuf_tensor("sg_wt0", [128, D], BF16) as wt0, \
                     nc.sbuf_tensor("sg_wt1", [128, D], BF16) as wt1, \
                     nc.sbuf_tensor("sg_ab0", [128, D], BF16) as ab0, \
                     nc.sbuf_tensor("sg_ab1", [128, D], BF16) as ab1, \
                     nc.sbuf_tensor("sg_ar", [1, D], F32) as ar:
                    wt, ab = [wt0, wt1], [ab0, ab1]
                    mm0 = mm.n
                    vx_done = []
                    psgate = state["psgate"]
                    wg8 = state.get("wload_gate")
                    if wg8 is not None:
                        el.wait("gpsimd", wg8[0], wg8[1])
                    for i in range(NC):
                        if i >= 2:
                            el.wait("gpsimd", vx, vx_done[i - 2])
                        el.op("gpsimd", lambda E, i=i: E.dma_start(
                            out=wt[i % 2][:], in_=w0t[i * 128:(i + 1) * 128, :]),
                            incs=[(cld, 16)])
                        wld = cld.n
                        el.wait("scalar", cld, wld)
                        if i >= 2:
                            el.wait("scalar", mm, mm0 + i - 1)
                        el.op("scalar", lambda E, i=i: E.activation(
                            ab[i % 2][:], wt[i % 2][:], AF.Abs), incs=[(ax, 1)])
                        abs_ax = ax.n
                        el.wait("vector", cld, wld)
                        el.op("vector", lambda E, i=i: E.tensor_scalar(
                            bigB[:, i * D:(i + 1) * D], wt[i % 2][:], 0.0, 2.0,
                            ALU.is_ge, ALU.mult), incs=[(vx, 1)])
                        el.wait("vector", vx)
                        el.op("vector", lambda E, i=i: E.tensor_scalar(
                            bigB[:, i * D:(i + 1) * D], bigB[:, i * D:(i + 1) * D],
                            -1.0, None, ALU.add), incs=[(vx, 1)])
                        vx_done.append(vx.n)
                        el.wait("tensor", ax, abs_ax)
                        for blk in range(NBK):
                            if i == 0:
                                g = psgate[blk]
                                if g is not None:
                                    el.wait("tensor", g[0], g[1])
                            el.op("tensor", lambda E, i=i, blk=blk: E.matmul(
                                ps_blk[0:1, blk * 512:(blk + 1) * 512],
                                ones_b[:, 0:1], ab[i % 2][:, blk * 512:(blk + 1) * 512],
                                start=(i == 0), stop=(i == NC - 1)),
                                incs=([(mm, 1)] if blk == NBK - 1 else ()))
                    el.wait("vector", mm)
                    el.op("vector", lambda E: E.tensor_scalar(
                        ar[0:1, :], ps_blk[0:1, :], INV_D, None, ALU.mult), incs=[(vx, 1)])
                    el.wait("tensor", vx)
                    el.wait("tensor", sld, consts_sld)
                    for blk in range(NBK):
                        el.op("tensor", lambda E, blk=blk: E.matmul(
                            ps_blk[:, blk * 512:(blk + 1) * 512],
                            ones_f[0:1, 0:128], ar[0:1, blk * 512:(blk + 1) * 512],
                            start=True, stop=True),
                            incs=([(mm, 1)] if blk == NBK - 1 else ()))
                    el.wait("vector", mm)
                    for blk in range(NBK):
                        el.op("vector", lambda E, blk=blk: E.tensor_copy(
                            alpha_bc[:, blk * 512:(blk + 1) * 512],
                            ps_blk[:, blk * 512:(blk + 1) * 512]),
                            incs=[(ev, 1)])
                        state["psgate"][blk] = (ev, ev.n)
                    el.flush()

            phase_barrier(P)
            with nc.sbuf_tensor(tag + "_xr0", [128, D], F32) as xr0, \
                 nc.sbuf_tensor(tag + "_xr1", [128, D], F32) as xr1, \
                 nc.sbuf_tensor(tag + "_r0", [128, D], F32) as r0, \
                 nc.sbuf_tensor(tag + "_r1", [128, D], F32) as r1, \
                 nc.sbuf_tensor(tag + "_xo0", [128, D], F32) as xo0, \
                 nc.sbuf_tensor(tag + "_xo1", [128, D], F32) as xo1, \
                 nc.sbuf_tensor(tag + "_smt", [128, 8], F32) as smt:
                xr, rr, xo = [xr0, xr1], [r0, r1], [xo0, xo1]
                tb = xo
                sm = smalls_of(smt)
                psgate = state["psgate"]
                sq_axs = []
                store_ssts = []
                for t in range(NT):
                    if t >= 2:
                        el.wait("sync", ax, sq_axs[t - 2])
                    el.op("sync", lambda E, t=t: E.dma_start(
                        out=xr[t % 2][:], in_=xin[e][t * 128:(t + 1) * 128, :]),
                        incs=[(sld, 16)])
                    xr_ld = sld.n

                    mm_t0 = mm.n
                    for ob in range(NBK):
                        g = psgate[ob]
                        if g is not None:
                            el.wait("tensor", g[0], g[1])
                        for i in range(NC):
                            el.op("tensor", lambda E, t=t, ob=ob, i=i: E.matmul(
                                ps_blk[:, ob * 512:(ob + 1) * 512],
                                bigA[:, i * S + t * 128: i * S + t * 128 + 128],
                                bigB[:, i * D + ob * 512: i * D + ob * 512 + 512],
                                start=(i == 0), stop=(i == NC - 1)),
                                incs=([(mm, 1)] if i == NC - 1 else ()))
                        psgate[ob] = (ev, ev.n + 2 * (ob + 1))

                    el.wait("vector", sld, xr_ld)
                    if t >= 2:
                        el.wait("vector", sst, store_ssts[t - 2])
                    el.op("vector", lambda E, t=t: E.tensor_tensor(
                        rr[t % 2][:], xr[t % 2][:], bias_f[:], ALU.add), incs=[(vx, 1)])
                    el.wait("vector", vx)
                    for ob in range(NBK):
                        el.wait("vector", mm, mm_t0 + ob + 1)
                        el.op("vector", lambda E, t=t, ob=ob: E.tensor_tensor(
                            tb[t % 2][:, ob * 512:(ob + 1) * 512],
                            ps_blk[:, ob * 512:(ob + 1) * 512],
                            alpha_bc[:, ob * 512:(ob + 1) * 512], ALU.mult),
                            incs=[(ev, 1)])
                        el.wait("vector", ev)
                        el.op("vector", lambda E, t=t, ob=ob: E.tensor_tensor(
                            rr[t % 2][:, ob * 512:(ob + 1) * 512],
                            tb[t % 2][:, ob * 512:(ob + 1) * 512],
                            rr[t % 2][:, ob * 512:(ob + 1) * 512], ALU.add),
                            incs=[(ev, 1)])
                    ev_r = ev.n

                    vx_apply, ax_sq = emit_layernorm(
                        P, rr[t % 2][:], xr[t % 2][:], xo[t % 2][:], sm, ev_r)
                    sq_axs.append(ax_sq)
                    el.wait("sync", vx, vx_apply)
                    el.op("sync", lambda E, t=t: E.dma_start(
                        out=xout[e][t * 128:(t + 1) * 128, :], in_=xo[t % 2][:]),
                        incs=[(sst, 16)])
                    store_ssts.append(sst.n)
                el.flush()
        state["mm_gate_bigA"] = mm.n
        state["elem_gates"] = [(sst, sst.n), (vx, vx.n), (ax, ax.n)]


def build_ffn1_program():
    P = Prog("ffn1")
    nc = P.nc
    INTER = 8192
    xin = nc.dram_tensor("xin", [2, S, D], F32, kind="ExternalInput").ap()
    wgu = nc.dram_tensor("wgu", [D, 2 * INTER], F32, kind="ExternalInput").ap()
    wdn = nc.dram_tensor("wdn", [INTER, D], F32, kind="ExternalInput").ap()
    eye_in = nc.dram_tensor("eye", [128, 128], F32, kind="ExternalInput").ap()
    xout = nc.dram_tensor("xout", [2, S, D], F32, kind="ExternalOutput").ap()

    bigA = P.sb("bigA", [128, NC * S], BF16)
    eye_b = P.sb("eye_b", [128, 128], BF16)
    ps_blk = P.ps("ps_blk", [128, 2048], F32)
    ps_tp = [P.ps("ps_tp0", [128, 512], BF16), P.ps("ps_tp1", [128, 512], BF16)]

    wgu_r = wgu.rearrange("(c p) o -> p c o", p=128)
    wdn_r = wdn.rearrange("(c p) o -> p c o", p=128)

    block = P.begin()
    el = P.el
    cld, sld, sst = P.sem("cld"), P.sem("sld"), P.sem("sst")
    mm, tp, ev, vx, ax = P.sem("mm"), P.sem("tp"), P.sem("ev"), P.sem("vx"), P.sem("ax")

    el.op("gpsimd", lambda E: E.dma_start(out=eye_b[:], in_=eye_in[:, :]), incs=[(cld, 16)])
    consts_cld = cld.n
    el.flush()

    state = {"mm_gate_bigA": 0, "psgate": {0: None, 1: None, 2: None, 3: None},
             "elem_gates": ()}
    _emit_ffn1_body(P, xin, xout, wgu_r, wdn_r, bigA, eye_b, ps_blk, ps_tp,
                    consts_cld, state)

    el.wait("sync", sst)
    el.flush()
    return P


def _emit_ffn1_body(P, xin, xout, wgu_r, wdn_r, bigA, eye_b, ps_blk, ps_tp,
                    consts_cld, state):
    nc, el = P.nc, P.el
    INTER = 8192
    cld, sld, sst = P.sem("cld"), P.sem("sld"), P.sem("sst")
    mm, tp, ev, vx, ax = P.sem("mm"), P.sem("tp"), P.sem("ev"), P.sem("vx"), P.sem("ax")
    for e in range(2):
        tag = f"g{e}"
        phase_barrier(P)
        with nc.sbuf_tensor(tag + "_xb0", [128, D], BF16) as xb0, \
             nc.sbuf_tensor(tag + "_xb1", [128, D], BF16) as xb1:
            emit_build_xt(P, xin[e], bigA, [xb0, xb1], eye_b, ps_tp,
                          state["mm_gate_bigA"], consts_cld, gates=state["elem_gates"])
            el.flush()

        for q8 in range(8):
            with nc.sbuf_tensor(f"{tag}q{q8}_ht", [128, 64 * 256], BF16) as ht8:
                phase_barrier(P)
                with nc.sbuf_tensor(f"{tag}q{q8}_wg0", [128, NC * 512], BF16) as wg0, \
                     nc.sbuf_tensor(f"{tag}q{q8}_wg1", [128, NC * 512], BF16) as wg1, \
                     nc.sbuf_tensor(f"{tag}q{q8}_wu0", [128, NC * 512], BF16) as wu0, \
                     nc.sbuf_tensor(f"{tag}q{q8}_wu1", [128, NC * 512], BF16) as wu1, \
                     nc.sbuf_tensor(f"{tag}q{q8}_sg", [128, 512], F32) as sg_f, \
                     nc.sbuf_tensor(f"{tag}q{q8}_hb", [128, 512], BF16) as h_bf:
                    wg, wu = [wg0, wg1], [wu0, wu1]
                    psgate = state["psgate"]
                    mm0 = mm.n
                    h_vxs = []
                    silu_axs = []
                    wg8 = state.get("wload_gate")
                    if wg8 is not None:
                        el.wait("gpsimd", wg8[0], wg8[1])
                    for pb in range(16):
                        if pb >= 2:
                            el.wait("gpsimd", mm, mm0 + (pb - 1) * 4)
                        el.op("gpsimd", lambda E, pb=pb: E.dma_start(
                            out=wg[pb % 2][:], in_=wgu_r[:, :, pb * 512:(pb + 1) * 512]),
                            incs=[(cld, 16)])
                        el.op("gpsimd", lambda E, pb=pb: E.dma_start(
                            out=wu[pb % 2][:],
                            in_=wgu_r[:, :, INTER + pb * 512: INTER + (pb + 1) * 512]),
                            incs=[(cld, 16)])
                        wld = cld.n
                        el.wait("tensor", cld, wld)
                        for tl in range(2):
                            t = q8 * 2 + tl
                            bg, bu = 2 * tl, 2 * tl + 1
                            it = pb * 2 + tl
                            g = psgate[bg]
                            if g is not None:
                                el.wait("tensor", g[0], g[1])
                            psgate[bg] = (ax, ax.n + len(silu_axs) - len(silu_axs) + 1)
                            for i in range(NC):
                                el.op("tensor", lambda E, pb=pb, tl=tl, t=t, bg=bg, i=i: E.matmul(
                                    ps_blk[:, bg * 512:(bg + 1) * 512],
                                    bigA[:, i * S + t * 128: i * S + t * 128 + 128],
                                    wg[pb % 2][:, i * 512:(i + 1) * 512],
                                    start=(i == 0), stop=(i == NC - 1)),
                                    incs=([(mm, 1)] if i == NC - 1 else ()))
                            mm_g = mm.n
                            g = psgate[bu]
                            if g is not None:
                                el.wait("tensor", g[0], g[1])
                            for i in range(NC):
                                el.op("tensor", lambda E, pb=pb, tl=tl, t=t, bu=bu, i=i: E.matmul(
                                    ps_blk[:, bu * 512:(bu + 1) * 512],
                                    bigA[:, i * S + t * 128: i * S + t * 128 + 128],
                                    wu[pb % 2][:, i * 512:(i + 1) * 512],
                                    start=(i == 0), stop=(i == NC - 1)),
                                    incs=([(mm, 1)] if i == NC - 1 else ()))
                            mm_u = mm.n

                            el.wait("scalar", mm, mm_g)
                            if h_vxs:
                                el.wait("scalar", vx, h_vxs[-1])
                            el.op("scalar", lambda E, bg=bg: E.activation(
                                sg_f[:], ps_blk[:, bg * 512:(bg + 1) * 512], AF.Silu),
                                incs=[(ax, 1)])
                            silu_axs.append(ax.n)
                            psgate[bg] = (ax, ax.n)
                            el.wait("vector", ax, ax.n)
                            el.wait("vector", mm, mm_u)
                            if it >= 1:
                                el.wait("vector", tp)
                            el.op("vector", lambda E, bu=bu: E.tensor_tensor(
                                h_bf[:], sg_f[:], ps_blk[:, bu * 512:(bu + 1) * 512],
                                ALU.mult), incs=[(vx, 1)])
                            h_vxs.append(vx.n)
                            psgate[bu] = (vx, vx.n)

                            transpose_group(
                                P, 4,
                                src_ap=lambda j: h_bf[:, j * 128:(j + 1) * 128],
                                dst_ap=lambda j, pb=pb, tl=tl: ht8[:, (pb * 4 + j) * 256 + tl * 128:
                                                                    (pb * 4 + j) * 256 + tl * 128 + 128],
                                eye_b=eye_b, ps_tp=ps_tp,
                                pre_t=[(vx, vx.n)], pre_v=())
                    gu_end_ev = ev.n
                    el.flush()

                phase_barrier(P)
                with nc.sbuf_tensor(f"{tag}q{q8}_wd0", [128, 16 * 512], BF16) as wd0, \
                     nc.sbuf_tensor(f"{tag}q{q8}_wd1", [128, 16 * 512], BF16) as wd1, \
                     nc.sbuf_tensor(f"{tag}q{q8}_wd2", [128, 16 * 512], BF16) as wd2, \
                     nc.sbuf_tensor(f"{tag}q{q8}_wd3", [128, 16 * 512], BF16) as wd3, \
                     nc.sbuf_tensor(f"{tag}q{q8}_xr0", [128, D], F32) as xr0, \
                     nc.sbuf_tensor(f"{tag}q{q8}_xr1", [128, D], F32) as xr1, \
                     nc.sbuf_tensor(f"{tag}q{q8}_r0", [128, D], F32) as r0, \
                     nc.sbuf_tensor(f"{tag}q{q8}_r1", [128, D], F32) as r1, \
                     nc.sbuf_tensor(f"{tag}q{q8}_smt", [128, 8], F32) as smt:
                    xr, rr = [xr0, xr1], [r0, r1]
                    xo = rr
                    sm = smalls_of(smt)
                    psgate = state["psgate"]
                    wdq = [wd0, wd1, wd2, wd3]
                    wg8 = state.get("wload_gate")
                    if wg8 is not None:
                        el.wait("gpsimd", wg8[0], wg8[1])
                    for tl in range(2):
                        t = q8 * 2 + tl
                        el.op("sync", lambda E, e=e, t=t, tl=tl: E.dma_start(
                            out=xr[tl][:], in_=xin[e][t * 128:(t + 1) * 128, :]),
                            incs=[(sld, 16)])
                    xr_ld = sld.n
                    resid_evs = {}
                    qmarks = {0: None, 1: None, 2: None, 3: None}
                    for ob in range(NBK):
                        wld0 = cld.n
                        for qf in range(4):
                            if qmarks[qf] is not None:
                                el.wait("gpsimd", mm, qmarks[qf])
                            el.op("gpsimd", lambda E, ob=ob, qf=qf: E.dma_start(
                                out=wdq[qf][:],
                                in_=wdn_r[:, qf * 16:(qf + 1) * 16, ob * 512:(ob + 1) * 512]),
                                incs=[(cld, 16)])
                        for tl in range(2):
                            t = q8 * 2 + tl
                            bank = 2 * (ob % 2) + tl
                            g = psgate[bank]
                            if g is not None:
                                el.wait("tensor", g[0], g[1])
                            for c in range(64):
                                if tl == 0 and c % 16 == 0:
                                    el.wait("tensor", cld, wld0 + 16 * (c // 16 + 1))
                                inc = []
                                if tl == 1 and c % 16 == 15:
                                    inc = [(mm, 1)]
                                elif tl == 0 and c == 63:
                                    inc = [(mm, 1)]
                                el.op("tensor", lambda E, ob=ob, tl=tl, c=c: E.matmul(
                                    ps_blk[:, (2 * (ob % 2) + tl) * 512:(2 * (ob % 2) + tl) * 512 + 512],
                                    ht8[:, c * 256 + tl * 128: c * 256 + tl * 128 + 128],
                                    wdq[c // 16][:, (c % 16) * 512:(c % 16) * 512 + 512],
                                    start=(c == 0), stop=(c == 63)),
                                    incs=inc)
                                if tl == 1 and c % 16 == 15:
                                    qmarks[c // 16] = mm.n
                            mm_d = mm.n
                            el.wait("vector", mm, mm_d)
                            if ob == 0 and tl == 0:
                                el.wait("vector", sld, xr_ld)
                            el.op("vector", lambda E, tl=tl, ob=ob, bank=bank: E.tensor_tensor(
                                rr[tl][:, ob * 512:(ob + 1) * 512],
                                ps_blk[:, bank * 512:(bank + 1) * 512],
                                xr[tl][:, ob * 512:(ob + 1) * 512], ALU.add),
                                incs=[(ev, 1)])
                            resid_evs[(tl, ob)] = ev.n
                            psgate[bank] = (ev, ev.n)
                    for tl in range(2):
                        t = q8 * 2 + tl
                        ev_r = max(resid_evs[(tl, ob)] for ob in range(NBK))
                        vx_apply, ax_sq = emit_layernorm(
                            P, rr[tl][:], xr[tl][:], xo[tl][:], sm, ev_r)
                        el.wait("sync", vx, vx_apply)
                        el.op("sync", lambda E, e=e, t=t, tl=tl: E.dma_start(
                            out=xout[e][t * 128:(t + 1) * 128, :], in_=xo[tl][:]),
                            incs=[(sst, 16)])
                    el.flush()
        state["mm_gate_bigA"] = mm.n
        state["elem_gates"] = [(sst, sst.n), (vx, vx.n), (ax, ax.n)]


def build_fused_program(debug_relay=False):
    P = Prog("fused")
    nc = P.nc
    INTER = 8192
    xin = nc.dram_tensor("xin", [2, S, D], F32, kind="ExternalInput").ap()
    wq_ap = nc.dram_tensor("wq", [D, D], F32, kind="ExternalInput").ap()
    wk_ap = nc.dram_tensor("wk", [D, D], F32, kind="ExternalInput").ap()
    wv_ap = nc.dram_tensor("wv", [D, D], F32, kind="ExternalInput").ap()
    wo_ap = nc.dram_tensor("wo", [D, D], F32, kind="ExternalInput").ap()
    w0t = nc.dram_tensor("w0t", [D, D], F32, kind="ExternalInput").ap()
    bias_in = nc.dram_tensor("bias_bc", [128, D], F32, kind="ExternalInput").ap()
    wqkv = nc.dram_tensor("wqkv", [D, 3 * D], F32, kind="ExternalInput").ap()
    wo1_ap = nc.dram_tensor("wo1", [D, D], F32, kind="ExternalInput").ap()
    wgu = nc.dram_tensor("wgu", [D, 2 * INTER], F32, kind="ExternalInput").ap()
    wdn = nc.dram_tensor("wdn", [INTER, D], F32, kind="ExternalInput").ap()
    eye_in = nc.dram_tensor("eye", [128, 128], F32, kind="ExternalInput").ap()
    mask_in = nc.dram_tensor("mask", [128, 128], F32, kind="ExternalInput").ap()
    mask01_in = nc.dram_tensor("mask01", [128, 128], F32, kind="ExternalInput").ap()
    ones_in = nc.dram_tensor("ones", [128, 128], F32, kind="ExternalInput").ap()
    xout = nc.dram_tensor("xout", [2, S, D], F32, kind="ExternalOutput").ap()
    rk = "ExternalOutput" if debug_relay else "Internal"
    x1_i = nc.dram_tensor("x1_i", [2, S, D], F32, kind=rk).ap()
    x2_i = nc.dram_tensor("x2_i", [2, S, D], F32, kind=rk).ap()
    x3_i = nc.dram_tensor("x3_i", [2, S, D], F32, kind=rk).ap()
    qt_hbm = nc.dram_tensor("qt_i", [D, S], BF16, kind="Internal").ap()
    kt_hbm = nc.dram_tensor("kt_i", [D, S], BF16, kind="Internal").ap()
    v_hbm = nc.dram_tensor("v_i", [S, D], BF16, kind="Internal").ap()

    bigA = P.sb("bigA", [128, NC * S], BF16)
    eye_b = P.sb("eye_b", [128, 128], BF16)
    mask_f = P.sb("mask_f", [128, 128], F32)
    mask01_b = P.sb("mask01_b", [128, 128], BF16)
    ps_blk = P.ps("ps_blk", [128, 2048], F32)
    ps_tp = [P.ps("ps_tp0", [128, 512], BF16), P.ps("ps_tp1", [128, 512], BF16)]
    ps_ctx = [P.ps("ps_ctx0", [128, 512], F32), P.ps("ps_ctx1", [128, 512], F32)]

    block = P.begin()
    el = P.el
    cld, sld, sst = P.sem("cld"), P.sem("sld"), P.sem("sst")
    mm, tp, ev, vx, ax = P.sem("mm"), P.sem("tp"), P.sem("ev"), P.sem("vx"), P.sem("ax")

    el.op("sync", lambda E: E.dma_start(out=mask_f[:], in_=mask_in[:, :]), incs=[(sld, 16)])
    el.op("gpsimd", lambda E: E.dma_start(out=eye_b[:], in_=eye_in[:, :]), incs=[(cld, 16)])
    el.op("gpsimd", lambda E: E.dma_start(out=mask01_b[:], in_=mask01_in[:, :]), incs=[(cld, 16)])
    consts_sld, consts_cld = sld.n, cld.n

    w0_c = nc.dram_tensor("w0_c", [D, D], BF16, kind="Internal").ap()
    wqkv_c = nc.dram_tensor("wqkv_c", [D, 3 * D], BF16, kind="Internal").ap()
    wo1_c = nc.dram_tensor("wo1_c", [D, D], BF16, kind="Internal").ap()
    wgu_c = nc.dram_tensor("wgu_c", [D, 2 * INTER], BF16, kind="Internal").ap()
    wdn_c = nc.dram_tensor("wdn_c", [INTER, D], BF16, kind="Internal").ap()
    prep = P.sem("prep")
    P.nobar.add("prep")
    for dst, src_ap, nch in [(w0_c, w0t, 2), (wqkv_c, wqkv, 4), (wo1_c, wo1_ap, 2),
                             (wgu_c, wgu, 8), (wdn_c, wdn, 8)]:
        rows = dst.shape[0] // nch
        for j in range(nch):
            el.op("gpsimd", lambda E, dst=dst, src_ap=src_ap, j=j, rows=rows: E.dma_start(
                out=dst[j * rows:(j + 1) * rows, :], in_=src_ap[j * rows:(j + 1) * rows, :]),
                incs=[(prep, 16)])
    el.flush()
    wload_gate = (prep, prep.n)
    wgu_r = wgu_c.rearrange("(c p) o -> p c o", p=128)
    wdn_r = wdn_c.rearrange("(c p) o -> p c o", p=128)

    state = {"mm_gate_bigA": 0, "ctx_gate_tp": 0, "dbg_xt": None, "dbg_ctx": None,
             "psgate": {0: None, 1: None, 2: None, 3: None}, "elem_gates": (),
             "mask01_b": mask01_b}

    with nc.sbuf_tensor("at0_bigB", [128, NT * D], BF16) as bigB:
        for e in range(2):
            _emit_attn_elem(P, e, xin[e], [wq_ap, wk_ap, wv_ap], wo_ap, x1_i[e],
                            qt_hbm, kt_hbm, v_hbm, bigA, bigB, eye_b, mask_f,
                            ps_blk, ps_tp, ps_ctx, consts_sld, consts_cld, state,
                            tagp="a0")

    phase_barrier(P)
    state["wload_gate"] = wload_gate
    with nc.sbuf_tensor("f0_bigB", [128, NC * D], BF16) as bigB, \
         nc.sbuf_tensor("f0_ones_b", [128, 128], BF16) as ones_b, \
         nc.sbuf_tensor("f0_ones_f", [128, 128], F32) as ones_f, \
         nc.sbuf_tensor("f0_bias_f", [128, D], F32) as bias_f, \
         nc.sbuf_tensor("f0_alpha", [128, D], F32) as alpha_bc:
        el.op("sync", lambda E: E.dma_start(out=bias_f[:], in_=bias_in[:, :]), incs=[(sld, 16)])
        el.op("sync", lambda E: E.dma_start(out=ones_f[:], in_=ones_in[:, :]), incs=[(sld, 16)])
        el.op("gpsimd", lambda E: E.dma_start(out=ones_b[:], in_=ones_in[:, :]), incs=[(cld, 16)])
        c_sld, c_cld = sld.n, cld.n
        el.flush()
        _emit_ffn0_body(P, x1_i, x2_i, w0_c, bigA, bigB, eye_b, ones_b, ones_f,
                        bias_f, alpha_bc, ps_blk, ps_tp, c_sld, c_cld, state)

    phase_barrier(P)
    state["ctx_gate_tp"] = 0
    with nc.sbuf_tensor("at1_bigB", [128, NT * D], BF16) as bigB:
        w_aps1 = [wqkv_c[:, 0:D], wqkv_c[:, D:2 * D], wqkv_c[:, 2 * D:3 * D]]
        for e in range(2):
            _emit_attn_elem(P, e, x2_i[e], w_aps1, wo1_c, x3_i[e],
                            qt_hbm, kt_hbm, v_hbm, bigA, bigB, eye_b, mask_f,
                            ps_blk, ps_tp, ps_ctx, consts_sld, consts_cld, state,
                            tagp="a1")

    phase_barrier(P)
    _emit_ffn1_body(P, x3_i, xout, wgu_r, wdn_r, bigA, eye_b, ps_blk, ps_tp,
                    consts_cld, state)

    el.wait("sync", sst)
    el.flush()
    return P


_EYE = np.eye(128, dtype=np.float32)
_MASK = np.where(np.arange(128)[None, :] <= np.arange(128)[:, None],
                 0.0, -1.0e38).astype(np.float32)
_MASK01 = np.where(np.arange(128)[None, :] <= np.arange(128)[:, None],
                   1.0, 0.0).astype(np.float32)
_ONES = np.ones((128, 128), dtype=np.float32)
_NCORES = 8


def _launch(nc, per_core, shared):
    in_maps = []
    for c in range(_NCORES):
        m = dict(shared)
        for k, v in per_core.items():
            m[k] = v[c]
        in_maps.append(m)
    res = run_bass_kernel_spmd(nc, in_maps, core_ids=list(range(_NCORES)))
    return [r["xout"] for r in res.results]


def kernel(L0_attn_x, L0_attn_Wq, L0_attn_Wk, L0_attn_Wv, L0_attn_Wo,
           L0_ffn_weight, L0_ffn_bias, L1_attn_w_qkv, L1_attn_w_o,
           L1_ffn_w_gate_up, L1_ffn_w_down):
    x = np.ascontiguousarray(np.asarray(L0_attn_x, dtype=np.float32))
    f = lambda a: np.ascontiguousarray(np.asarray(a, dtype=np.float32))
    ft = lambda a: np.ascontiguousarray(np.asarray(a, dtype=np.float32).T)

    wq, wk, wv, wo = f(L0_attn_Wq), f(L0_attn_Wk), f(L0_attn_Wv), f(L0_attn_Wo)
    w0t = ft(L0_ffn_weight)
    bias_bc = np.ascontiguousarray(
        np.broadcast_to(np.asarray(L0_ffn_bias, np.float32)[None, :], (128, D)))
    wqkv = ft(L1_attn_w_qkv)
    wo1 = ft(L1_attn_w_o)
    wgu = ft(L1_ffn_w_gate_up)
    wdn = ft(L1_ffn_w_down)

    shards = [np.ascontiguousarray(x[2 * c:2 * c + 2]) for c in range(_NCORES)]

    P = build_fused_program()
    xs = _launch(P.finish(), {"xin": shards},
                 {"wq": wq, "wk": wk, "wv": wv, "wo": wo, "w0t": w0t,
                  "bias_bc": bias_bc, "wqkv": wqkv, "wo1": wo1,
                  "wgu": wgu, "wdn": wdn,
                  "eye": _EYE, "mask": _MASK, "mask01": _MASK01, "ones": _ONES})
    out = np.concatenate([np.asarray(a, np.float32) for a in xs], axis=0)
    return out
